# version 03
# baseline (speedup 1.0000x reference)
import numpy as np
import ml_dtypes
import contextlib
import concourse.bass as bass
import concourse.mybir as mybir
from concourse.bass_utils import run_bass_kernel_spmd

F32 = mybir.dt.float32
BF16 = mybir.dt.bfloat16
AF = mybir.ActivationFunctionType
ALU = mybir.AluOpType
BF = ml_dtypes.bfloat16

NCORES = 8
RPC = 2048
NCHUNK = 4
R = 512
DT = 32
D = 4096
FT = 86
DFF = 11008
FH = 43
L = 4096
EPS = 1e-5
SC32 = float(2.0 ** -32)
INV_D = 1.0 / 4096.0

_HC = {}


def host_constants():
    if _HC:
        return _HC
    l = np.arange(L, dtype=np.float64)
    om = np.exp(-2j * np.pi * l / L)
    W = (1.0 - om) / (1.0 + om)
    c2 = 2.0 / (1.0 + om)
    _HC["g2re"] = np.ascontiguousarray(
        np.broadcast_to(W.real.astype(np.float32), (64, L)))
    _HC["g2im"] = np.ascontiguousarray(
        np.broadcast_to(W.imag.astype(np.float32), (64, L)))
    _HC["cre"] = np.ascontiguousarray(c2.real.astype(np.float32).reshape(64, 64))
    _HC["cim"] = np.ascontiguousarray(c2.imag.astype(np.float32).reshape(64, 64))
    n = np.arange(64)
    w64 = np.exp(2j * np.pi * np.outer(n, n) / 64.0)
    _HC["w64re"] = w64.real.astype(np.float32).copy()
    _HC["w64im"] = w64.imag.astype(np.float32).copy()
    _HC["w64imn"] = (-w64.imag).astype(np.float32).copy()
    tw = np.exp(2j * np.pi * np.outer(n, n) / float(L))
    _HC["twre"] = tw.real.astype(np.float32).copy()
    _HC["twim"] = tw.imag.astype(np.float32).copy()
    _HC["onesf"] = np.ones((128, 128), dtype=np.float32)
    _HC["idn"] = np.eye(128).astype(BF)
    _HC["jfl"] = np.eye(128)[::-1].copy().astype(BF)
    return _HC


_WC = {}


def host_weights(gate_w, up_w, down_w, gate_s, up_s, down_s):
    key = (id(gate_w), id(up_w), id(down_w))
    if key in _WC:
        return _WC[key]

    def gu(w):
        wt = np.asarray(w, np.int32).astype(BF).T
        return np.ascontiguousarray(
            wt.reshape(DT, 128, FT, 128).transpose(2, 1, 0, 3)).reshape(
                FT, 128, DT * 128)
    g = gu(gate_w)
    u = gu(up_w)
    guw = np.ascontiguousarray(
        np.stack([g, u], axis=1))
    dwT = np.asarray(down_w, np.int32).astype(BF).T
    dwt = np.ascontiguousarray(
        dwT.reshape(2, FH, 128, DT, 128).transpose(3, 0, 2, 1, 4)).reshape(
            DT, 2, 128, FH * 128)
    gs = np.ascontiguousarray(np.asarray(gate_s, np.float32).reshape(FT, 128).T)
    us = np.ascontiguousarray(np.asarray(up_s, np.float32).reshape(FT, 128).T)
    ds = np.ascontiguousarray(np.asarray(down_s, np.float32).reshape(DT, 128).T)
    out = dict(guw=guw, dwt=dwt, gs=gs, us=us, ds=ds)
    _WC.clear()
    _WC[key] = out
    return out


class Sched:

    def __init__(self):
        self.labels = {}
        self.cnt = {}
        self.sems = {}
        self.mode = "count"
        self.waited = {}

    def reset(self):
        self.cnt = {}
        self.waited = {}


def build_nc(probe=False):
    nc = bass.Bass("TRN2", target_bir_lowering=False, debug=False)

    d_x = nc.dram_tensor("x", [D, RPC], F32, kind="ExternalInput").ap()
    d_out = nc.dram_tensor("out", [D, RPC], F32, kind="ExternalOutput").ap()
    d_guw = nc.dram_tensor("guw", [FT, 2, 128, DT * 128], BF16,
                           kind="ExternalInput").ap()
    d_dwt = nc.dram_tensor("dwt", [DT, 2, 128, FH * 128], BF16,
                           kind="ExternalInput").ap()
    d_gs = nc.dram_tensor("gs", [128, FT], F32, kind="ExternalInput").ap()
    d_us = nc.dram_tensor("us", [128, FT], F32, kind="ExternalInput").ap()
    d_ds = nc.dram_tensor("ds", [128, DT], F32, kind="ExternalInput").ap()
    sml = {}
    for nm in ["lamre", "lamim", "ctre", "ctim", "pre_", "pim", "qre", "qim",
               "bp"]:
        sml[nm] = nc.dram_tensor(nm, [64, 1], F32, kind="ExternalInput").ap()
    d_vctre = nc.dram_tensor("vctre", [64, 64], F32, kind="ExternalInput").ap()
    d_vctim = nc.dram_tensor("vctim", [64, 64], F32, kind="ExternalInput").ap()
    d_dd = nc.dram_tensor("dd", [1, 1], F32, kind="ExternalInput").ap()
    d_ls = nc.dram_tensor("ls", [1, 1], F32, kind="ExternalInput").ap()
    d_g2re = nc.dram_tensor("g2re", [64, L], F32, kind="ExternalInput").ap()
    d_g2im = nc.dram_tensor("g2im", [64, L], F32, kind="ExternalInput").ap()
    cns = {}
    for nm in ["cre", "cim", "w64re", "w64im", "w64imn", "twre", "twim"]:
        cns[nm] = nc.dram_tensor(nm, [64, 64], F32, kind="ExternalInput").ap()
    d_onesf = nc.dram_tensor("onesf", [128, 128], F32, kind="ExternalInput").ap()
    d_idn = nc.dram_tensor("idn", [128, 128], BF16, kind="ExternalInput").ap()
    d_jfl = nc.dram_tensor("jfl", [128, 128], BF16, kind="ExternalInput").ap()

    d_xmid = nc.dram_tensor("xmid", [D, RPC], F32, kind="Internal").ap()
    d_kscr = nc.dram_tensor("kscr", [8, L], F32, kind="Internal").ap()
    d_kpad = nc.dram_tensor("kpad", [4224], F32, kind="Internal").ap()
    if probe:
        d_pk = nc.dram_tensor("pk", [1, 4224], F32, kind="ExternalOutput").ap()
        d_pt = nc.dram_tensor("pt", [128, L], F32, kind="ExternalOutput").ap()
        d_pkv = nc.dram_tensor("pkv", [8, L], F32, kind="ExternalOutput").ap()

    es = contextlib.ExitStack()
    E = es.enter_context

    R1 = E(nc.sbuf_tensor("R1", [128, DT * R], F32))
    R2 = E(nc.sbuf_tensor("R2", [128, DT * R], BF16))
    R3h = E(nc.sbuf_tensor("R3h", [128, (FT - 64) * R], BF16))
    wgu = E(nc.sbuf_tensor("wgu", [128, 2 * 4096], BF16))
    wdn = E(nc.sbuf_tensor("wdn", [128, 2 * FH * 128], BF16))
    tsb = E(nc.sbuf_tensor("tsb", [128, DT * 128], BF16))
    sidn = E(nc.sbuf_tensor("sidn", [128, 128], BF16))
    sj = E(nc.sbuf_tensor("sj", [128, 128], BF16))
    onesf = E(nc.sbuf_tensor("onesf_s", [128, 128], F32))
    hm2 = E(nc.sbuf_tensor("hm2", [128, 256], BF16))
    sgs = E(nc.sbuf_tensor("sgs", [128, FT], F32))
    sus = E(nc.sbuf_tensor("sus", [128, FT], F32))
    sds = E(nc.sbuf_tensor("sds", [128, DT], F32))
    m_rep = E(nc.sbuf_tensor("m_rep", [128, R], F32))
    istd_rep = E(nc.sbuf_tensor("istd_rep", [128, R], F32))
    tnorm = E(nc.sbuf_tensor("tnorm", [128, R], F32))
    sqs = E(nc.sbuf_tensor("sqs", [128, 2 * R], F32))
    silu_sb = E(nc.sbuf_tensor("silu_sb", [128, 2 * R], F32))
    xms = E(nc.sbuf_tensor("xms", [128, 2 * R], F32))
    outs = E(nc.sbuf_tensor("outs", [128, 2 * R], F32))
    dcol = E(nc.sbuf_tensor("dcol", [128, 1], F32))
    epsc = E(nc.sbuf_tensor("epsc", [128, 1], F32))

    KA = R1[:]
    S0 = lambda a, b: KA[:, a:b]
    R2f = R2[:].bitcast(F32)
    c64 = R2f[0:64, 0:448]
    svt = R2f[0:64, 448:576]
    prm = R2f[0:64, 576:608]
    sV = R2f[:, 608:616]
    kc = R2f[0:64, 616:1128]
    ar = R2f[0:64, 1128:1768]
    st1 = R2f[0:1, 1768:1776]
    sdi = R2[:, 16256:16384]

    def A_(i):
        return ar[:, 64 * i:64 * (i + 1)]

    def KC(i):
        return kc[:, 64 * i:64 * (i + 1)]

    pS0 = E(nc.psum_tensor("pS0", [128, 512], F32))
    pS1 = E(nc.psum_tensor("pS1", [128, 512], F32))
    pG = [E(nc.psum_tensor("pG0", [128, 512], F32)),
          E(nc.psum_tensor("pG1", [128, 512], F32))]
    pU = [E(nc.psum_tensor("pU0", [128, 512], F32)),
          E(nc.psum_tensor("pU1", [128, 512], F32))]
    pD = [E(nc.psum_tensor("pD0", [128, 512], F32)),
          E(nc.psum_tensor("pD1", [128, 512], F32))]

    sch = Sched()
    for s in ["ldc", "w", "x", "xm", "out", "kp", "kc", "hm",
              "pe", "dve", "act", "pr"]:
        sch.sems[s] = E(nc.semaphore("s_" + s))

    ctx = {"emit": False}

    def I(fn, sem=None, label=None, amt=1):
        inst = fn() if ctx["emit"] else None
        if sem is not None:
            v = sch.cnt.get(sem, 0) + amt
            sch.cnt[sem] = v
            if label is not None and sch.mode == "count":
                sch.labels[(sem,) + label] = v
            if ctx["emit"] and inst is not None:
                inst.then_inc(sch.sems[sem], amt)
        return inst

    def mark(sem, label):
        if sch.mode == "count":
            sch.labels[(sem,) + label] = sch.cnt.get(sem, 0)

    def W(eng, key, sem, label):
        if not ctx["emit"]:
            return
        v = sch.labels[(sem,) + label]
        wkey = (key, sem)
        if sch.waited.get(wkey, -1) >= v:
            return
        sch.waited[wkey] = v
        eng.wait_ge(sch.sems[sem], v)

    def WV(eng, key, sem, value):
        if not ctx["emit"]:
            return
        wkey = (key, sem)
        if sch.waited.get(wkey, -1) >= value:
            return
        sch.waited[wkey] = value
        eng.wait_ge(sch.sems[sem], value)

    def h_tile(f):
        if f < 64:
            return R1[:].bitcast(BF16)[:, R * f:R * (f + 1)]
        return R3h[:, R * (f - 64):R * (f - 63)]

    def body_sync(q):
        loads = [
            (sgs[:], d_gs), (sus[:], d_us), (sds[:], d_ds),
            (sidn[:], d_idn), (sj[:], d_jfl), (onesf[:], d_onesf),
            (prm[:, 0:1], sml["lamre"]), (prm[:, 1:2], sml["lamim"]),
            (prm[:, 2:3], sml["ctre"]), (prm[:, 3:4], sml["ctim"]),
            (prm[:, 4:5], sml["pre_"]), (prm[:, 5:6], sml["pim"]),
            (prm[:, 6:7], sml["qre"]), (prm[:, 7:8], sml["qim"]),
            (prm[:, 8:9], sml["bp"]),
            (svt[:, 0:64], d_vctre), (svt[:, 64:128], d_vctim),
            (st1[:, 0:1], d_dd), (st1[:, 1:2], d_ls),
            (c64[:, 0:64], cns["cre"]), (c64[:, 64:128], cns["cim"]),
            (c64[:, 128:192], cns["w64re"]), (c64[:, 192:256], cns["w64im"]),
            (c64[:, 256:320], cns["w64imn"]), (c64[:, 320:384], cns["twre"]),
            (c64[:, 384:448], cns["twim"]),
            (S0(4096, 8192)[0:64, :], d_g2re),
            (S0(8192, 12288)[0:64, :], d_g2im),
        ]
        for dst, src in loads:
            I(lambda d=dst, s=src: q.dma_start(out=d, in_=s), "ldc", None, 16)
        mark("ldc", ("all",))
        if probe:
            return

        for c in range(NCHUNK):
            for f in range(FT):
                for w in range(2):
                    j = 2 * (FT * c + f) + w
                    if j >= 2:
                        W(q, "SQ", "pe", ("gu", j - 2))
                    I(lambda f=f, w=w: q.dma_start(
                        out=wgu[:, 4096 * w:4096 * (w + 1)], in_=d_guw[f, w]),
                        "w", ("wg", c, f, w), 16)
            for dt in range(DT):
                for hh in range(2):
                    k = 2 * (DT * c + dt) + hh
                    if k >= 2:
                        W(q, "SQ", "pe", ("dn", k - 2))
                    I(lambda dt=dt, hh=hh: q.dma_start(
                        out=wdn[:, 5504 * hh:5504 * (hh + 1)],
                        in_=d_dwt[dt, hh]),
                        "w", ("wd", c, dt, hh), 16)

    def body_gq(q):
        W(q, "GQ", "dve", ("kv", 7))
        I(lambda: q.dma_start(out=d_kscr, in_=S0(4096, 8192)[0:8, :]),
          "kc", None, 16)
        WV(q, "GQ", "kc", 16)
        for i in range(8):
            I(lambda i=i: q.dma_start(
                out=KC(i), in_=d_kscr[i].rearrange("(a b) -> a b", b=64)),
                "kc", None, 16)
        mark("kc", ("all",))
        W(q, "GQ", "dve", ("zt",))
        I(lambda: q.dma_start(
            out=d_kpad[0:4224].rearrange("(a b) -> a b", a=1),
            in_=S0(0, 4224)[0:1, :]), "kp", None, 16)
        W(q, "GQ", "dve", ("ksb",))
        WV(q, "GQ", "kp", 16)
        I(lambda: q.dma_start(
            out=d_kpad[127:4223].rearrange("(a b) -> a b", b=64),
            in_=A_(9)), "kp", None, 16)
        WV(q, "GQ", "kp", 32)
        if probe:
            I(lambda: q.dma_start(
                out=d_pk, in_=d_kpad[0:4224].rearrange("(a b) -> a b", a=1)),
                "pr", None, 16)
            I(lambda: q.dma_start(out=d_pkv, in_=d_kscr), "pr", None, 16)
        for m in range(DT):
            if m >= 2:
                W(q, "GQ", "pe", ("tprime", m - 2))
            ap_h = bass.AP(d_kpad.tensor, 128 * m, [[1, 128], [1, 128]])
            I(lambda m=m, ap_h=ap_h: q.dma_start(
                out=hm2[:, 128 * (m % 2):128 * (m % 2) + 128], in_=ap_h),
                "hm", ("hm", m), 16)
        if probe:
            W(q, "GQ", "dve", ("tsb", DT - 1))
            I(lambda: q.dma_start(out=d_pt, in_=tsb[:]), "pr", None, 16)
            if ctx["emit"]:
                q.wait_ge(sch.sems["pr"], sch.cnt["pr"])
            return

        for c in range(NCHUNK):
            if c == 0:
                W(q, "GQ", "pe", ("kmat",))
            else:
                W(q, "GQ", "pe", ("down", c - 1, DT - 1))
            for qq in range(4):
                I(lambda c=c, qq=qq: q.dma_start(
                    out=R1[:, 4096 * qq:4096 * (qq + 1)]
                    .rearrange("p (t r) -> p t r", r=R),
                    in_=d_x[1024 * qq:1024 * (qq + 1), R * c:R * (c + 1)]
                    .rearrange("(t p) r -> p t r", p=128)),
                    "x", ("x", c, qq), 16)
            W(q, "GQ", "dve", ("xn2", c, DT - 1))
            for qq in range(4):
                I(lambda c=c, qq=qq: q.dma_start(
                    out=d_xmid[1024 * qq:1024 * (qq + 1), R * c:R * (c + 1)]
                    .rearrange("(t p) r -> p t r", p=128),
                    in_=R1[:, 4096 * qq:4096 * (qq + 1)]
                    .rearrange("p (t r) -> p t r", r=R)),
                    "xm", None, 16)
            mark("xm", ("xmo", c))
            W(q, "GQ", "xm", ("xmo", c))
            def xmi(dt):
                if dt >= 2:
                    W(q, "GQ", "dve", ("f2evac", c, dt - 2))
                I(lambda c=c, dt=dt: q.dma_start(
                    out=xms[:, R * (dt % 2):R * (dt % 2) + R],
                    in_=d_xmid[128 * dt:128 * (dt + 1), R * c:R * (c + 1)]),
                    "xm", ("xmi", c, dt), 16)
            xmi(0)
            xmi(1)
            for dt in range(DT):
                if dt + 2 < DT:
                    xmi(dt + 2)
                W(q, "GQ", "dve", ("f2evac", c, dt))
                I(lambda c=c, dt=dt: q.dma_start(
                    out=d_out[128 * dt:128 * (dt + 1), R * c:R * (c + 1)],
                    in_=outs[:, R * (dt % 2):R * (dt % 2) + R]),
                    "out", ("out", c, dt), 16)
        if ctx["emit"]:
            q.wait_ge(sch.sems["out"], sch.cnt["out"])

    def body_te(te):
        W(te, "TE", "ldc", ("all",))
        I(lambda: te.matmul(pS0[0:64, 0:1], svt[:, 0:64], prm[:, 8:9],
                            start=True, stop=True), "pe", ("bre",))
        I(lambda: te.matmul(pS0[0:64, 1:2], svt[:, 64:128], prm[:, 8:9],
                            start=True, stop=True), "pe", ("bim",))
        W(te, "TE", "act", ("exp",))
        I(lambda: te.matmul(pS0[0:64, 2:3], onesf[0:1, 0:64], st1[:, 2:3],
                            start=True, stop=True), "pe", ("s2b",))
        I(lambda: te.matmul(pS0[0:128, 3:4], onesf[0:1, 0:128], st1[:, 0:1],
                            start=True, stop=True), "pe", ("db",))
        W(te, "TE", "dve", ("inv",))
        W(te, "TE", "dve", ("v_done",))
        for k in range(8):
            if k >= 1:
                W(te, "TE", "dve", ("kv", k - 1))
            I(lambda k=k: te.matmul(pS1[0:8, :], sV,
                                    S0(512 * k, 512 * (k + 1)),
                                    start=True, stop=True),
                "pe", ("cauchy", k))
        W(te, "TE", "dve", ("ar_done",))
        I(lambda: te.matmul(pS1[0:64, 0:64], A_(0), c64[:, 128:192],
                            start=True, stop=False), "pe")
        I(lambda: te.matmul(pS1[0:64, 0:64], A_(1), c64[:, 256:320],
                            start=False, stop=True), "pe")
        I(lambda: te.matmul(pS1[0:64, 64:128], A_(0), c64[:, 192:256],
                            start=True, stop=False), "pe")
        I(lambda: te.matmul(pS1[0:64, 64:128], A_(1), c64[:, 128:192],
                            start=False, stop=True), "pe", ("s1",))
        W(te, "TE", "dve", ("tw_done",))
        I(lambda: te.matmul(pS1[0:64, 128:192], c64[:, 128:192], A_(6),
                            start=True, stop=False), "pe")
        I(lambda: te.matmul(pS1[0:64, 128:192], c64[:, 256:320], A_(7),
                            start=False, stop=True), "pe", ("kmat",))
        W(te, "TE", "act", ("sdi",))
        for m in range(DT):
            WV(te, "TE", "hm", 16 * (m + 1))
            if m >= 2:
                W(te, "TE", "dve", ("tsb", m - 2))
            I(lambda m=m: te.matmul(
                pG[m % 2][:, 0:128], sj[:],
                hm2[:, 128 * (m % 2):128 * (m % 2) + 128],
                start=True, stop=(m != 0)),
                "pe", ("tprime", m) if m != 0 else None)
            if m == 0:
                I(lambda: te.matmul(pG[0][:, 0:128], sidn[:], sdi,
                                    start=False, stop=True),
                    "pe", ("tprime", 0))
        if probe:
            return

        for c in range(NCHUNK):
            if c == 0:
                W(te, "TE", "dve", ("tsb_done",))
            else:
                W(te, "TE", "dve", ("var2", c - 1))
            for t in range(DT):
                W(te, "TE", "x", ("x", c, t // 8))
                I(lambda c=c, t=t: te.matmul(
                    pS0[:], onesf[:], R1[:, R * t:R * (t + 1)],
                    start=(t == 0), stop=(t == DT - 1)),
                    "pe", ("sxmm1", c) if t == DT - 1 else None)
                W(te, "TE", "act", ("sq1", c, t))
                I(lambda c=c, t=t: te.matmul(
                    pS1[:], onesf[:], sqs[:, R * (t % 2):R * (t % 2) + R],
                    start=(t == 0), stop=(t == DT - 1)),
                    "pe", ("sqmm1", c, t))
            for i in range(DT):
                if i >= 2:
                    W(te, "TE", "dve", ("cevac", c, i - 2))
                elif c >= 1:
                    W(te, "TE", "dve", ("f2evac", c - 1, 30 + i))
                for j in range(i + 1):
                    W(te, "TE", "dve", ("xn1", c, j))
                    I(lambda c=c, i=i, j=j: te.matmul(
                        pD[i % 2][:],
                        tsb[:, 128 * (i - j):128 * (i - j) + 128],
                        R2[:, R * j:R * (j + 1)],
                        start=(j == 0), stop=(j == i)),
                        "pe", ("conv", c, i) if j == i else None)
            W(te, "TE", "dve", ("var1", c))
            for t in range(DT):
                W(te, "TE", "dve", ("cevac", c, t))
                I(lambda c=c, t=t: te.matmul(
                    pS0[:], onesf[:], R1[:, R * t:R * (t + 1)],
                    start=(t == 0), stop=(t == DT - 1)),
                    "pe", ("sxmm2", c) if t == DT - 1 else None)
                W(te, "TE", "act", ("sq2", c, t))
                I(lambda c=c, t=t: te.matmul(
                    pS1[:], onesf[:], sqs[:, R * (t % 2):R * (t % 2) + R],
                    start=(t == 0), stop=(t == DT - 1)),
                    "pe", ("sqmm2", c, t))
            W(te, "TE", "dve", ("xn2", c, DT - 1))
            for f in range(FT):
                jg = 2 * (FT * c + f)
                W(te, "TE", "w", ("wg", c, f, 0))
                if f >= 2:
                    W(te, "TE", "act", ("silu", c, f - 2))
                elif c >= 1:
                    W(te, "TE", "act", ("silu", c - 1, 84 + f))
                for t in range(DT):
                    I(lambda f=f, t=t: te.matmul(
                        pG[f % 2][:], wgu[:, 128 * t:128 * (t + 1)],
                        R2[:, R * t:R * (t + 1)],
                        start=(t == 0), stop=(t == DT - 1)),
                        "pe", ("gate", c, f) if t == DT - 1 else None)
                mark("pe", ("gu", jg))
                W(te, "TE", "w", ("wg", c, f, 1))
                if f >= 2:
                    W(te, "TE", "dve", ("h", c, f - 2))
                elif c >= 1:
                    W(te, "TE", "dve", ("h", c - 1, 84 + f))
                for t in range(DT):
                    I(lambda f=f, t=t: te.matmul(
                        pU[f % 2][:], wgu[:, 4096 + 128 * t:4096 + 128 * (t + 1)],
                        R2[:, R * t:R * (t + 1)],
                        start=(t == 0), stop=(t == DT - 1)),
                        "pe", ("up", c, f) if t == DT - 1 else None)
                mark("pe", ("gu", jg + 1))
            W(te, "TE", "dve", ("h", c, FT - 1))
            for dt in range(DT):
                if dt >= 2:
                    W(te, "TE", "dve", ("f2evac", c, dt - 2))
                for hh in range(2):
                    kk = 2 * (DT * c + dt) + hh
                    W(te, "TE", "w", ("wd", c, dt, hh))
                    for l in range(FH):
                        fi = 43 * hh + l
                        I(lambda dt=dt, hh=hh, l=l, fi=fi: te.matmul(
                            pD[dt % 2][:],
                            wdn[:, 5504 * hh + 128 * l:5504 * hh + 128 * (l + 1)],
                            h_tile(fi),
                            start=(fi == 0), stop=(fi == FT - 1)),
                            "pe", ("down", c, dt) if fi == FT - 1 else None)
                    mark("pe", ("dn", kk))

    def body_v(v):
        def C(fn, sem="dve", label=None):
            I(fn, sem, label)
            if ctx["emit"]:
                v.wait_ge(sch.sems["dve"], sch.cnt["dve"])

        def ts(out, in0, s1, s2, o1, o2=None):
            if o2 is None:
                return lambda: v.tensor_scalar(out, in0, s1, s2, o1)
            return lambda: v.tensor_scalar(out, in0, s1, s2, o1, o2)

        W(v, "V", "ldc", ("all",))
        C(lambda: v.memset(epsc[:], EPS))
        W(v, "V", "pe", ("bre",))
        C(lambda: v.tensor_copy(prm[:, 10:11], pS0[0:64, 0:1]))
        W(v, "V", "pe", ("bim",))
        C(lambda: v.tensor_copy(prm[:, 11:12], pS0[0:64, 1:2]))
        W(v, "V", "pe", ("s2b",))
        C(ts(prm[:, 22:23], pS0[0:64, 2:3], 2.0 * SC32, None, ALU.mult, None))
        C(ts(prm[:, 23:24], prm[:, 0:1], SC32, None, ALU.mult, None))
        C(ts(prm[:, 24:25], prm[:, 1:2], SC32, None, ALU.mult, None))
        W(v, "V", "pe", ("db",))
        C(lambda: v.tensor_copy(dcol[:], pS0[0:128, 3:4]), "dve", ("dcol",))

        def cmul(ore, oim, a_re, a_im, b_re, b_im):
            C(lambda: v.tensor_mul(prm[:, 20:21], prm[:, a_re:a_re + 1],
                                   prm[:, b_re:b_re + 1]))
            C(lambda: v.tensor_mul(prm[:, 21:22], prm[:, a_im:a_im + 1],
                                   prm[:, b_im:b_im + 1]))
            C(lambda: v.tensor_add(prm[:, ore:ore + 1], prm[:, 20:21],
                                   prm[:, 21:22]))
            C(lambda: v.tensor_mul(prm[:, 20:21], prm[:, a_re:a_re + 1],
                                   prm[:, b_im:b_im + 1]))
            C(lambda: v.tensor_mul(prm[:, 21:22], prm[:, a_im:a_im + 1],
                                   prm[:, b_re:b_re + 1]))
            C(lambda: v.tensor_sub(prm[:, oim:oim + 1], prm[:, 20:21],
                                   prm[:, 21:22]))
        cmul(12, 13, 2, 3, 10, 11)
        cmul(14, 15, 2, 3, 4, 5)
        cmul(16, 17, 6, 7, 10, 11)
        cmul(18, 19, 6, 7, 4, 5)
        for i in range(4):
            re = 12 + 2 * i
            im = 13 + 2 * i
            C(ts(sV[0:64, 2 * i:2 * i + 1], prm[:, re:re + 1], SC32, None,
                 ALU.mult, None))
            C(ts(sV[64:128, 2 * i:2 * i + 1], prm[:, im:im + 1], SC32, None,
                 ALU.mult, None))
            C(ts(sV[0:64, 2 * i + 1:2 * i + 2], prm[:, im:im + 1], SC32, None,
                 ALU.mult, None))
            C(ts(sV[64:128, 2 * i + 1:2 * i + 2], prm[:, re:re + 1], -SC32,
                 None, ALU.mult, None))
        mark("dve", ("v_done",))

        g2re = S0(4096, 8192)[0:64, :]
        g2im = S0(8192, 12288)[0:64, :]
        s3 = S0(12288, 16384)[0:64, :]
        i0 = S0(0, 4096)[0:64, :]
        i1 = S0(0, 4096)[64:128, :]
        C(ts(g2re, g2re, prm[:, 22:23], prm[:, 23:24], ALU.mult, ALU.subtract))
        C(ts(g2im, g2im, prm[:, 22:23], prm[:, 24:25], ALU.mult, ALU.subtract))
        C(lambda: v.tensor_mul(s3, g2re, g2re))
        C(lambda: v.tensor_mul(i0, g2im, g2im))
        C(lambda: v.tensor_add(i0, i0, s3))
        C(lambda: v.reciprocal(s3, i0))
        C(lambda: v.tensor_mul(i0, g2re, s3))
        C(lambda: v.tensor_mul(i1, g2im, s3),
          "dve", ("inv",))
        for k in range(8):
            W(v, "V", "pe", ("cauchy", k))
            C(lambda k=k: v.tensor_copy(
                S0(4096 + 512 * k, 4096 + 512 * (k + 1))[0:8, :], pS1[0:8, :]),
                "dve", ("kv", k))

        W(v, "V", "kc", ("all",))
        C(ts(A_(2), KC(6), 1.0, None, ALU.add, None))
        C(lambda: v.tensor_mul(A_(3), A_(2), A_(2)))
        C(lambda: v.tensor_mul(A_(4), KC(7), KC(7)))
        C(lambda: v.tensor_add(A_(3), A_(3), A_(4)))
        C(lambda: v.reciprocal(A_(3), A_(3)))
        C(lambda: v.tensor_mul(A_(4), KC(2), KC(4)))
        C(lambda: v.tensor_mul(A_(5), KC(3), KC(5)))
        C(lambda: v.tensor_sub(A_(4), A_(4), A_(5)))
        C(lambda: v.tensor_mul(A_(5), KC(2), KC(5)))
        C(lambda: v.tensor_mul(A_(6), KC(3), KC(4)))
        C(lambda: v.tensor_add(A_(5), A_(5), A_(6)))
        C(lambda: v.tensor_mul(A_(6), A_(4), A_(2)))
        C(lambda: v.tensor_mul(A_(7), A_(5), KC(7)))
        C(lambda: v.tensor_add(A_(6), A_(6), A_(7)))
        C(lambda: v.tensor_mul(A_(6), A_(6), A_(3)))
        C(lambda: v.tensor_mul(A_(7), A_(5), A_(2)))
        C(lambda: v.tensor_mul(A_(8), A_(4), KC(7)))
        C(lambda: v.tensor_sub(A_(7), A_(7), A_(8)))
        C(lambda: v.tensor_mul(A_(7), A_(7), A_(3)))
        C(lambda: v.tensor_sub(A_(2), KC(0), A_(6)))
        C(lambda: v.tensor_sub(A_(3), KC(1), A_(7)))
        C(lambda: v.tensor_mul(A_(4), c64[:, 0:64], A_(2)))
        C(lambda: v.tensor_mul(A_(5), c64[:, 64:128], A_(3)))
        C(lambda: v.tensor_sub(A_(0), A_(4), A_(5)))
        C(lambda: v.tensor_mul(A_(4), c64[:, 0:64], A_(3)))
        C(lambda: v.tensor_mul(A_(5), c64[:, 64:128], A_(2)))
        C(lambda: v.tensor_add(A_(1), A_(4), A_(5)),
          "dve", ("ar_done",))
        C(lambda: v.memset(S0(0, 4224)[0:1, :], 0.0), "dve", ("zt",))
        W(v, "V", "pe", ("s1",))
        C(lambda: v.tensor_copy(A_(4), pS1[0:64, 0:64]))
        C(lambda: v.tensor_copy(A_(5), pS1[0:64, 64:128]))
        C(lambda: v.tensor_mul(A_(6), A_(4), c64[:, 320:384]))
        C(lambda: v.tensor_mul(A_(7), A_(5), c64[:, 384:448]))
        C(lambda: v.tensor_sub(A_(6), A_(6), A_(7)))
        C(lambda: v.tensor_mul(A_(7), A_(4), c64[:, 384:448]))
        C(lambda: v.tensor_mul(A_(8), A_(5), c64[:, 320:384]))
        C(lambda: v.tensor_add(A_(7), A_(7), A_(8)),
          "dve", ("tw_done",))
        W(v, "V", "pe", ("kmat",))
        C(ts(A_(9), pS1[0:64, 128:192], 1.0 / L, None, ALU.mult, None),
          "dve", ("ksb",))
        for m in range(DT):
            W(v, "V", "pe", ("tprime", m))
            C(lambda m=m: v.tensor_copy(
                tsb[:, 128 * m:128 * (m + 1)],
                pG[m % 2][:, 0:128]), "dve", ("tsb", m))
        mark("dve", ("tsb_done",))
        if probe:
            return

        for c in range(NCHUNK):
            def stats(which):
                W(v, "V", "pe", ("sxmm%d" % which, c))
                W(v, "V", "pe", ("sqmm%d" % which, c, DT - 1))
                C(ts(m_rep[:], pS0[:], INV_D, None, ALU.mult, None))
                C(ts(istd_rep[:], pS1[:], INV_D, None, ALU.mult, None))
                C(lambda: v.tensor_mul(tnorm[:], m_rep[:], m_rep[:]))
                C(lambda: v.tensor_sub(istd_rep[:], istd_rep[:], tnorm[:]),
                  "dve", ("var%d" % which, c))

            def normalize(which, lbl):
                for t in range(DT):
                    if t == 0:
                        W(v, "V", "act", ("istd%d" % which, c))
                        if which == 1 and c >= 1:
                            W(v, "V", "pe", ("up", c - 1, FT - 1))
                        if which == 2:
                            W(v, "V", "pe", ("conv", c, DT - 1))
                    C(lambda t=t: v.tensor_sub(
                        tnorm[:], R1[:, R * t:R * (t + 1)], m_rep[:]))
                    C(lambda t=t: v.tensor_mul(
                        R2[:, R * t:R * (t + 1)], tnorm[:], istd_rep[:]),
                        "dve", (lbl, c, t))

            stats(1)
            normalize(1, "xn1")
            for i in range(DT):
                W(v, "V", "pe", ("conv", c, i))
                C(lambda i=i: v.tensor_add(
                    R1[:, R * i:R * (i + 1)], pD[i % 2][:],
                    R1[:, R * i:R * (i + 1)]), "dve", ("cevac", c, i))
            stats(2)
            normalize(2, "xn2")
            for f in range(FT):
                W(v, "V", "pe", ("up", c, f))
                W(v, "V", "act", ("silu", c, f))
                if f == 0:
                    W(v, "V", "xm", ("xmo", c))
                C(lambda f=f: v.scalar_tensor_tensor(
                    h_tile(f), pU[f % 2][:], sus[:, f:f + 1],
                    silu_sb[:, R * (f % 2):R * (f % 2) + R],
                    ALU.mult, ALU.mult), "dve", ("h", c, f))
            for dt in range(DT):
                W(v, "V", "pe", ("down", c, dt))
                W(v, "V", "xm", ("xmi", c, dt))
                if dt >= 2:
                    W(v, "V", "out", ("out", c, dt - 2))
                C(lambda dt=dt: v.scalar_tensor_tensor(
                    outs[:, R * (dt % 2):R * (dt % 2) + R], pD[dt % 2][:],
                    sds[:, dt:dt + 1], xms[:, R * (dt % 2):R * (dt % 2) + R],
                    ALU.mult, ALU.add), "dve", ("f2evac", c, dt))

    def body_a(a):
        def C(fn, sem="act", label=None):
            I(fn, sem, label)
            if ctx["emit"]:
                a.wait_ge(sch.sems["act"], sch.cnt["act"])

        W(a, "A", "ldc", ("all",))
        C(lambda: a.activation(st1[:, 2:3], st1[:, 1:2], AF.Exp, scale=-1.0),
          "act", ("exp",))
        W(a, "A", "dve", ("dcol",))
        C(lambda: a.activation(sdi, sidn[:], AF.Copy, scale=dcol[:]),
          "act", ("sdi",))
        if probe:
            return
        for c in range(NCHUNK):
            for t in range(DT):
                W(a, "A", "x", ("x", c, t // 8))
                if t >= 2:
                    W(a, "A", "pe", ("sqmm1", c, t - 2))
                C(lambda t=t: a.activation(
                    sqs[:, R * (t % 2):R * (t % 2) + R],
                    R1[:, R * t:R * (t + 1)], AF.Square),
                    "act", ("sq1", c, t))
            W(a, "A", "dve", ("var1", c))
            C(lambda: a.activation(tnorm[:], istd_rep[:], AF.Ln, bias=epsc[:]))
            C(lambda: a.activation(istd_rep[:], tnorm[:], AF.Exp, scale=-0.5),
              "act", ("istd1", c))
            for t in range(DT):
                W(a, "A", "dve", ("cevac", c, t))
                if t >= 2:
                    W(a, "A", "pe", ("sqmm2", c, t - 2))
                C(lambda t=t: a.activation(
                    sqs[:, R * (t % 2):R * (t % 2) + R],
                    R1[:, R * t:R * (t + 1)], AF.Square),
                    "act", ("sq2", c, t))
            W(a, "A", "dve", ("var2", c))
            C(lambda: a.activation(tnorm[:], istd_rep[:], AF.Ln, bias=epsc[:]))
            C(lambda: a.activation(istd_rep[:], tnorm[:], AF.Exp, scale=-0.5),
              "act", ("istd2", c))
            for f in range(FT):
                W(a, "A", "pe", ("gate", c, f))
                if f >= 2:
                    W(a, "A", "dve", ("h", c, f - 2))
                elif c >= 1:
                    W(a, "A", "dve", ("h", c - 1, 84 + f))
                C(lambda f=f: a.activation(
                    silu_sb[:, R * (f % 2):R * (f % 2) + R], pG[f % 2][:],
                    AF.Silu, scale=sgs[:, f:f + 1]),
                    "act", ("silu", c, f))

    for mode in ("count", "emit"):
        sch.mode = mode
        sch.reset()
        ctx["emit"] = mode == "emit"
        if mode == "count":
            body_sync(None)
            body_gq(None)
            body_te(None)
            body_v(None)
            body_a(None)
        else:
            with nc.Block() as block:
                block.sync(body_sync)
                block.gpsimd(body_gq)
                block.tensor(body_te)
                block.vector(body_v)
                block.scalar(body_a)
    es.close()
    return nc


_NC_CACHE = {}


def stage_inputs(L0_attn_u, L0_attn_Lambda_ri, L0_attn_p_ri, L0_attn_q_ri,
                 L0_attn_Vc_ri, L0_attn_Ct_ri, L0_attn_Bp, L0_attn_D,
                 L0_attn_log_step, L0_ffn_gate_w, L0_ffn_gate_s, L0_ffn_up_w,
                 L0_ffn_up_s, L0_ffn_down_w, L0_ffn_down_s):
    hc = host_constants()
    hw = host_weights(L0_ffn_gate_w, L0_ffn_up_w, L0_ffn_down_w,
                      L0_ffn_gate_s, L0_ffn_up_s, L0_ffn_down_s)
    lam = np.asarray(L0_attn_Lambda_ri, np.float32)
    p_ri = np.asarray(L0_attn_p_ri, np.float32)
    q_ri = np.asarray(L0_attn_q_ri, np.float32)
    vc = np.asarray(L0_attn_Vc_ri, np.float32)
    ct = np.asarray(L0_attn_Ct_ri, np.float32).reshape(64, 2)
    common = dict(
        guw=hw["guw"], dwt=hw["dwt"], gs=hw["gs"], us=hw["us"], ds=hw["ds"],
        lamre=np.ascontiguousarray(lam[:, 0:1]),
        lamim=np.ascontiguousarray(lam[:, 1:2]),
        ctre=np.ascontiguousarray(ct[:, 0:1]),
        ctim=np.ascontiguousarray(ct[:, 1:2]),
        pre_=np.ascontiguousarray(p_ri[:, 0:1]),
        pim=np.ascontiguousarray(p_ri[:, 1:2]),
        qre=np.ascontiguousarray(q_ri[:, 0:1]),
        qim=np.ascontiguousarray(q_ri[:, 1:2]),
        vctre=np.ascontiguousarray(vc[:, :, 0].T),
        vctim=np.ascontiguousarray(vc[:, :, 1].T),
        bp=np.asarray(L0_attn_Bp, np.float32).reshape(64, 1),
        dd=np.asarray(L0_attn_D, np.float32).reshape(1, 1),
        ls=np.asarray(L0_attn_log_step, np.float32).reshape(1, 1),
        g2re=hc["g2re"], g2im=hc["g2im"], cre=hc["cre"], cim=hc["cim"],
        w64re=hc["w64re"], w64im=hc["w64im"], w64imn=hc["w64imn"],
        twre=hc["twre"], twim=hc["twim"], onesf=hc["onesf"],
        idn=hc["idn"], jfl=hc["jfl"],
    )
    xt = np.ascontiguousarray(
        np.asarray(L0_attn_u, np.float32).reshape(-1, D).T)
    return common, xt


def kernel(**inputs):
    common, xt = stage_inputs(**inputs)
    in_maps = []
    for c in range(NCORES):
        m = dict(common)
        m["x"] = np.ascontiguousarray(xt[:, RPC * c:RPC * (c + 1)])
        in_maps.append(m)
    if "main" not in _NC_CACHE:
        _NC_CACHE["main"] = build_nc(probe=False)
    nc = _NC_CACHE["main"]
    res = run_bass_kernel_spmd(nc, in_maps, core_ids=list(range(NCORES)))
    full = np.concatenate([r["out"] for r in res.results], axis=1)
    return np.ascontiguousarray(full.T).reshape(
        np.asarray(inputs["L0_attn_u"]).shape).astype(np.float32)


if __name__ == "__main__":
    nc = build_nc()
    print("instructions:", sum(1 for _ in nc.all_instructions()))


# version 05
# speedup vs baseline: 1.0483x; 1.0483x over previous
import numpy as np
import ml_dtypes
import contextlib
import concourse.bass as bass
import concourse.mybir as mybir
from concourse.bass_utils import run_bass_kernel_spmd

F32 = mybir.dt.float32
BF16 = mybir.dt.bfloat16
AF = mybir.ActivationFunctionType
ALU = mybir.AluOpType
BF = ml_dtypes.bfloat16

NCORES = 8
RPC = 2048
NCHUNK = 4
R = 512
DT = 32
D = 4096
FT = 86
DFF = 11008
FH = 43
L = 4096
EPS = 1e-5
SC32 = float(2.0 ** -32)
INV_D = 1.0 / 4096.0

_HC = {}


def host_constants():
    if _HC:
        return _HC
    l = np.arange(L, dtype=np.float64)
    om = np.exp(-2j * np.pi * l / L)
    W = (1.0 - om) / (1.0 + om)
    c2 = 2.0 / (1.0 + om)
    _HC["g2re"] = np.ascontiguousarray(
        np.broadcast_to(W.real.astype(np.float32), (64, L)))
    _HC["g2im"] = np.ascontiguousarray(
        np.broadcast_to(W.imag.astype(np.float32), (64, L)))
    _HC["cre"] = np.ascontiguousarray(c2.real.astype(np.float32).reshape(64, 64))
    _HC["cim"] = np.ascontiguousarray(c2.imag.astype(np.float32).reshape(64, 64))
    n = np.arange(64)
    w64 = np.exp(2j * np.pi * np.outer(n, n) / 64.0)
    _HC["w64re"] = w64.real.astype(np.float32).copy()
    _HC["w64im"] = w64.imag.astype(np.float32).copy()
    _HC["w64imn"] = (-w64.imag).astype(np.float32).copy()
    tw = np.exp(2j * np.pi * np.outer(n, n) / float(L))
    _HC["twre"] = tw.real.astype(np.float32).copy()
    _HC["twim"] = tw.imag.astype(np.float32).copy()
    _HC["onesf"] = np.ones((128, 128), dtype=np.float32)
    _HC["onesb"] = np.ones((128, 128), dtype=BF)
    _HC["idn"] = np.eye(128).astype(BF)
    _HC["jfl"] = np.eye(128)[::-1].copy().astype(BF)
    return _HC


_WC = {}


def host_weights(gate_w, up_w, down_w, gate_s, up_s, down_s):
    key = (id(gate_w), id(up_w), id(down_w))
    if key in _WC:
        return _WC[key]

    def gu(w):
        wt = np.asarray(w, np.int32).astype(BF).T
        return np.ascontiguousarray(
            wt.reshape(DT, 128, FT, 128).transpose(2, 1, 0, 3)).reshape(
                FT, 128, DT * 128)
    g = gu(gate_w)
    u = gu(up_w)
    guw = np.ascontiguousarray(
        np.stack([g, u], axis=1))
    dwT = np.asarray(down_w, np.int32).astype(BF).T
    dwt = np.ascontiguousarray(
        dwT.reshape(2, FH, 128, DT, 128).transpose(3, 0, 2, 1, 4)).reshape(
            DT, 2, 128, FH * 128)
    gs = np.ascontiguousarray(np.asarray(gate_s, np.float32).reshape(FT, 128).T)
    us = np.ascontiguousarray(np.asarray(up_s, np.float32).reshape(FT, 128).T)
    ds = np.ascontiguousarray(np.asarray(down_s, np.float32).reshape(DT, 128).T)
    out = dict(guw=guw, dwt=dwt, gs=gs, us=us, ds=ds)
    _WC.clear()
    _WC[key] = out
    return out


class Sched:

    def __init__(self):
        self.labels = {}
        self.cnt = {}
        self.sems = {}
        self.mode = "count"
        self.waited = {}

    def reset(self):
        self.cnt = {}
        self.waited = {}


def build_nc(probe=False):
    nc = bass.Bass("TRN2", target_bir_lowering=False, debug=False)

    d_x = nc.dram_tensor("x", [D, RPC], F32, kind="ExternalInput").ap()
    d_out = nc.dram_tensor("out", [D, RPC], F32, kind="ExternalOutput").ap()
    d_guw = nc.dram_tensor("guw", [FT, 2, 128, DT * 128], BF16,
                           kind="ExternalInput").ap()
    d_dwt = nc.dram_tensor("dwt", [DT, 2, 128, FH * 128], BF16,
                           kind="ExternalInput").ap()
    d_gs = nc.dram_tensor("gs", [128, FT], F32, kind="ExternalInput").ap()
    d_us = nc.dram_tensor("us", [128, FT], F32, kind="ExternalInput").ap()
    d_ds = nc.dram_tensor("ds", [128, DT], F32, kind="ExternalInput").ap()
    sml = {}
    for nm in ["lamre", "lamim", "ctre", "ctim", "pre_", "pim", "qre", "qim",
               "bp"]:
        sml[nm] = nc.dram_tensor(nm, [64, 1], F32, kind="ExternalInput").ap()
    d_vctre = nc.dram_tensor("vctre", [64, 64], F32, kind="ExternalInput").ap()
    d_vctim = nc.dram_tensor("vctim", [64, 64], F32, kind="ExternalInput").ap()
    d_dd = nc.dram_tensor("dd", [1, 1], F32, kind="ExternalInput").ap()
    d_ls = nc.dram_tensor("ls", [1, 1], F32, kind="ExternalInput").ap()
    d_g2re = nc.dram_tensor("g2re", [64, L], F32, kind="ExternalInput").ap()
    d_g2im = nc.dram_tensor("g2im", [64, L], F32, kind="ExternalInput").ap()
    cns = {}
    for nm in ["cre", "cim", "w64re", "w64im", "w64imn", "twre", "twim"]:
        cns[nm] = nc.dram_tensor(nm, [64, 64], F32, kind="ExternalInput").ap()
    d_onesf = nc.dram_tensor("onesf", [128, 128], F32, kind="ExternalInput").ap()
    d_onesb = nc.dram_tensor("onesb", [128, 128], BF16, kind="ExternalInput").ap()
    d_idn = nc.dram_tensor("idn", [128, 128], BF16, kind="ExternalInput").ap()
    d_jfl = nc.dram_tensor("jfl", [128, 128], BF16, kind="ExternalInput").ap()

    d_xmid = nc.dram_tensor("xmid", [D, RPC], F32, kind="Internal").ap()
    d_kscr = nc.dram_tensor("kscr", [8, L], F32, kind="Internal").ap()
    d_kpad = nc.dram_tensor("kpad", [4224], F32, kind="Internal").ap()
    if probe:
        d_pk = nc.dram_tensor("pk", [1, 4224], F32, kind="ExternalOutput").ap()
        d_pt = nc.dram_tensor("pt", [128, L], F32, kind="ExternalOutput").ap()
        d_pkv = nc.dram_tensor("pkv", [8, L], F32, kind="ExternalOutput").ap()

    es = contextlib.ExitStack()
    E = es.enter_context

    R1 = E(nc.sbuf_tensor("R1", [128, DT * R], F32))
    R2 = E(nc.sbuf_tensor("R2", [128, DT * R], BF16))
    R3h = E(nc.sbuf_tensor("R3h", [128, (FT - 64) * R], BF16))
    wgu = E(nc.sbuf_tensor("wgu", [128, 2 * 4096], BF16))
    wdn = E(nc.sbuf_tensor("wdn", [128, 2 * FH * 128], BF16))
    tsb = E(nc.sbuf_tensor("tsb", [128, DT * 128], BF16))
    sidn = E(nc.sbuf_tensor("sidn", [128, 128], BF16))
    sj = E(nc.sbuf_tensor("sj", [128, 128], BF16))
    onesf = E(nc.sbuf_tensor("onesf_s", [128, 128], F32))
    onesb = E(nc.sbuf_tensor("onesb_s", [128, 128], BF16))
    hm2 = E(nc.sbuf_tensor("hm2", [128, 256], BF16))
    sgs = E(nc.sbuf_tensor("sgs", [128, FT], F32))
    sus = E(nc.sbuf_tensor("sus", [128, FT], F32))
    sds = E(nc.sbuf_tensor("sds", [128, DT], F32))
    m_rep = E(nc.sbuf_tensor("m_rep", [128, R], F32))
    istd_rep = E(nc.sbuf_tensor("istd_rep", [128, R], F32))
    tnorm = E(nc.sbuf_tensor("tnorm", [128, 3 * R], F32))
    sc2 = E(nc.sbuf_tensor("sc2", [128, 2 * R], F32))
    sqs = E(nc.sbuf_tensor("sqs", [128, 2 * R], BF16))
    silu_sb = E(nc.sbuf_tensor("silu_sb", [128, 2 * R], F32))
    xms = E(nc.sbuf_tensor("xms", [128, 2 * R], F32))
    outs = E(nc.sbuf_tensor("outs", [128, 2 * R], F32))
    dcol = E(nc.sbuf_tensor("dcol", [128, 1], F32))
    epsc = E(nc.sbuf_tensor("epsc", [128, 1], F32))

    KA = R1[:]
    S0 = lambda a, b: KA[:, a:b]
    R2f = R2[:].bitcast(F32)
    c64 = R2f[0:64, 0:448]
    svt = R2f[0:64, 448:576]
    prm = R2f[0:64, 576:608]
    sV = R2f[:, 608:616]
    kc = R2f[0:64, 616:1128]
    ar = R2f[0:64, 1128:1768]
    st1 = R2f[0:1, 1768:1776]
    sdi = R2[:, 16256:16384]

    def A_(i):
        return ar[:, 64 * i:64 * (i + 1)]

    def KC(i):
        return kc[:, 64 * i:64 * (i + 1)]

    pS0 = E(nc.psum_tensor("pS0", [128, 512], F32))
    pS1 = E(nc.psum_tensor("pS1", [128, 512], F32))
    pG = [E(nc.psum_tensor("pG0", [128, 512], F32)),
          E(nc.psum_tensor("pG1", [128, 512], F32))]
    pU = [E(nc.psum_tensor("pU0", [128, 512], F32)),
          E(nc.psum_tensor("pU1", [128, 512], F32))]
    pD = [E(nc.psum_tensor("pD0", [128, 512], F32)),
          E(nc.psum_tensor("pD1", [128, 512], F32))]

    sch = Sched()
    for s in ["ldc", "w", "x", "xm", "out", "kp", "kc", "hm",
              "pe", "dve", "act", "gp", "pr"]:
        sch.sems[s] = E(nc.semaphore("s_" + s))

    ctx = {"emit": False}

    def I(fn, sem=None, label=None, amt=1):
        inst = fn() if ctx["emit"] else None
        if sem is not None:
            v = sch.cnt.get(sem, 0) + amt
            sch.cnt[sem] = v
            if label is not None and sch.mode == "count":
                sch.labels[(sem,) + label] = v
            if ctx["emit"] and inst is not None:
                inst.then_inc(sch.sems[sem], amt)
        return inst

    def mark(sem, label):
        if sch.mode == "count":
            sch.labels[(sem,) + label] = sch.cnt.get(sem, 0)

    def W(eng, key, sem, label):
        if not ctx["emit"]:
            return
        v = sch.labels[(sem,) + label]
        wkey = (key, sem)
        if sch.waited.get(wkey, -1) >= v:
            return
        sch.waited[wkey] = v
        eng.wait_ge(sch.sems[sem], v)

    def WV(eng, key, sem, value):
        if not ctx["emit"]:
            return
        wkey = (key, sem)
        if sch.waited.get(wkey, -1) >= value:
            return
        sch.waited[wkey] = value
        eng.wait_ge(sch.sems[sem], value)

    def h_tile(f):
        if f < 64:
            return R1[:].bitcast(BF16)[:, R * f:R * (f + 1)]
        return R3h[:, R * (f - 64):R * (f - 63)]

    def body_sync(q):
        loads = [
            (sgs[:], d_gs), (sus[:], d_us), (sds[:], d_ds),
            (sidn[:], d_idn), (sj[:], d_jfl), (onesf[:], d_onesf),
            (onesb[:], d_onesb),
            (prm[:, 0:1], sml["lamre"]), (prm[:, 1:2], sml["lamim"]),
            (prm[:, 2:3], sml["ctre"]), (prm[:, 3:4], sml["ctim"]),
            (prm[:, 4:5], sml["pre_"]), (prm[:, 5:6], sml["pim"]),
            (prm[:, 6:7], sml["qre"]), (prm[:, 7:8], sml["qim"]),
            (prm[:, 8:9], sml["bp"]),
            (svt[:, 0:64], d_vctre), (svt[:, 64:128], d_vctim),
            (st1[:, 0:1], d_dd), (st1[:, 1:2], d_ls),
            (c64[:, 0:64], cns["cre"]), (c64[:, 64:128], cns["cim"]),
            (c64[:, 128:192], cns["w64re"]), (c64[:, 192:256], cns["w64im"]),
            (c64[:, 256:320], cns["w64imn"]), (c64[:, 320:384], cns["twre"]),
            (c64[:, 384:448], cns["twim"]),
            (S0(4096, 8192)[0:64, :], d_g2re),
            (S0(8192, 12288)[0:64, :], d_g2im),
        ]
        for dst, src in loads:
            I(lambda d=dst, s=src: q.dma_start(out=d, in_=s), "ldc", None, 16)
        mark("ldc", ("all",))
        if probe:
            return

        for c in range(NCHUNK):
            for f in range(FT):
                for w in range(2):
                    j = 2 * (FT * c + f) + w
                    if j >= 2:
                        W(q, "SQ", "pe", ("gu", j - 2))
                    I(lambda f=f, w=w: q.dma_start(
                        out=wgu[:, 4096 * w:4096 * (w + 1)], in_=d_guw[f, w]),
                        "w", ("wg", c, f, w), 16)
            for dt in range(DT):
                for hh in range(2):
                    k = 2 * (DT * c + dt) + hh
                    if k >= 2:
                        W(q, "SQ", "pe", ("dn", k - 2))
                    I(lambda dt=dt, hh=hh: q.dma_start(
                        out=wdn[:, 5504 * hh:5504 * (hh + 1)],
                        in_=d_dwt[dt, hh]),
                        "w", ("wd", c, dt, hh), 16)

    def body_gq(q):
        W(q, "GQ", "dve", ("kv", 7))
        I(lambda: q.dma_start(out=d_kscr, in_=S0(4096, 8192)[0:8, :]),
          "kc", None, 16)
        WV(q, "GQ", "kc", 16)
        for i in range(8):
            I(lambda i=i: q.dma_start(
                out=KC(i), in_=d_kscr[i].rearrange("(a b) -> a b", b=64)),
                "kc", None, 16)
        mark("kc", ("all",))
        W(q, "GQ", "dve", ("zt",))
        I(lambda: q.dma_start(
            out=d_kpad[0:4224].rearrange("(a b) -> a b", a=1),
            in_=S0(0, 4224)[0:1, :]), "kp", None, 16)
        W(q, "GQ", "dve", ("ksb",))
        WV(q, "GQ", "kp", 16)
        I(lambda: q.dma_start(
            out=d_kpad[127:4223].rearrange("(a b) -> a b", b=64),
            in_=A_(9)), "kp", None, 16)
        WV(q, "GQ", "kp", 32)
        if probe:
            I(lambda: q.dma_start(
                out=d_pk, in_=d_kpad[0:4224].rearrange("(a b) -> a b", a=1)),
                "pr", None, 16)
            I(lambda: q.dma_start(out=d_pkv, in_=d_kscr), "pr", None, 16)
        for m in range(DT):
            if m >= 2:
                W(q, "GQ", "pe", ("tprime", m - 2))
            ap_h = bass.AP(d_kpad.tensor, 128 * m, [[1, 128], [1, 128]])
            I(lambda m=m, ap_h=ap_h: q.dma_start(
                out=hm2[:, 128 * (m % 2):128 * (m % 2) + 128], in_=ap_h),
                "hm", ("hm", m), 16)
        if probe:
            W(q, "GQ", "dve", ("tsb", DT - 1))
            I(lambda: q.dma_start(out=d_pt, in_=tsb[:]), "pr", None, 16)
            if ctx["emit"]:
                q.wait_ge(sch.sems["pr"], sch.cnt["pr"])
            return

        for c in range(NCHUNK):
            if c == 0:
                W(q, "GQ", "pe", ("kmat",))
            else:
                W(q, "GQ", "pe", ("down", c - 1, DT - 1))
            for qq in range(4):
                I(lambda c=c, qq=qq: q.dma_start(
                    out=R1[:, 4096 * qq:4096 * (qq + 1)]
                    .rearrange("p (t r) -> p t r", r=R),
                    in_=d_x[1024 * qq:1024 * (qq + 1), R * c:R * (c + 1)]
                    .rearrange("(t p) r -> p t r", p=128)),
                    "x", ("x", c, qq), 16)
            W(q, "GQ", "dve", ("xb2g", c, 7))
            for qq in range(4):
                I(lambda c=c, qq=qq: q.dma_start(
                    out=d_xmid[1024 * qq:1024 * (qq + 1), R * c:R * (c + 1)]
                    .rearrange("(t p) r -> p t r", p=128),
                    in_=R1[:, 4096 * qq:4096 * (qq + 1)]
                    .rearrange("p (t r) -> p t r", r=R)),
                    "xm", None, 16)
            mark("xm", ("xmo", c))
            for t in range(DT):
                W(q, "GQ", "dve", ("xn2s", c, t))
                if t == 0:
                    W(q, "GQ", "act", ("istd2", c))
                I(lambda c=c, t=t: q.tensor_mul(
                    R2[:, R * t:R * (t + 1)],
                    tnorm[:, R * (t % 2):R * (t % 2) + R], istd_rep[:]),
                    "gp", ("xn2m", c, t))
            W(q, "GQ", "xm", ("xmo", c))
            def xmi(dt):
                if dt >= 2:
                    W(q, "GQ", "dve", ("f2evac", c, dt - 2))
                I(lambda c=c, dt=dt: q.dma_start(
                    out=xms[:, R * (dt % 2):R * (dt % 2) + R],
                    in_=d_xmid[128 * dt:128 * (dt + 1), R * c:R * (c + 1)]),
                    "xm", ("xmi", c, dt), 16)
            xmi(0)
            xmi(1)
            for dt in range(DT):
                if dt + 2 < DT:
                    xmi(dt + 2)
                W(q, "GQ", "dve", ("f2evac", c, dt))
                I(lambda c=c, dt=dt: q.dma_start(
                    out=d_out[128 * dt:128 * (dt + 1), R * c:R * (c + 1)],
                    in_=outs[:, R * (dt % 2):R * (dt % 2) + R]),
                    "out", ("out", c, dt), 16)
        if ctx["emit"]:
            q.wait_ge(sch.sems["out"], sch.cnt["out"])

    def body_te(te):
        W(te, "TE", "ldc", ("all",))
        I(lambda: te.matmul(pS0[0:64, 0:1], svt[:, 0:64], prm[:, 8:9],
                            start=True, stop=True), "pe", ("bre",))
        I(lambda: te.matmul(pS0[0:64, 1:2], svt[:, 64:128], prm[:, 8:9],
                            start=True, stop=True), "pe", ("bim",))
        W(te, "TE", "act", ("exp",))
        I(lambda: te.matmul(pS0[0:64, 2:3], onesf[0:1, 0:64], st1[:, 2:3],
                            start=True, stop=True), "pe", ("s2b",))
        I(lambda: te.matmul(pS0[0:128, 3:4], onesf[0:1, 0:128], st1[:, 0:1],
                            start=True, stop=True), "pe", ("db",))
        W(te, "TE", "dve", ("inv",))
        W(te, "TE", "dve", ("v_done",))
        for k in range(8):
            if k >= 1:
                W(te, "TE", "dve", ("kv", k - 1))
            I(lambda k=k: te.matmul(pS1[0:8, :], sV,
                                    S0(512 * k, 512 * (k + 1)),
                                    start=True, stop=True),
                "pe", ("cauchy", k))
        W(te, "TE", "dve", ("ar_done",))
        I(lambda: te.matmul(pS1[0:64, 0:64], A_(0), c64[:, 128:192],
                            start=True, stop=False), "pe")
        I(lambda: te.matmul(pS1[0:64, 0:64], A_(1), c64[:, 256:320],
                            start=False, stop=True), "pe")
        I(lambda: te.matmul(pS1[0:64, 64:128], A_(0), c64[:, 192:256],
                            start=True, stop=False), "pe")
        I(lambda: te.matmul(pS1[0:64, 64:128], A_(1), c64[:, 128:192],
                            start=False, stop=True), "pe", ("s1",))
        W(te, "TE", "dve", ("tw_done",))
        I(lambda: te.matmul(pS1[0:64, 128:192], c64[:, 128:192], A_(6),
                            start=True, stop=False), "pe")
        I(lambda: te.matmul(pS1[0:64, 128:192], c64[:, 256:320], A_(7),
                            start=False, stop=True), "pe", ("kmat",))
        W(te, "TE", "act", ("sdi",))
        for m in range(DT):
            WV(te, "TE", "hm", 16 * (m + 1))
            if m >= 2:
                W(te, "TE", "dve", ("tsb", m - 2))
            I(lambda m=m: te.matmul(
                pG[m % 2][:, 0:128], sj[:],
                hm2[:, 128 * (m % 2):128 * (m % 2) + 128],
                start=True, stop=(m != 0)),
                "pe", ("tprime", m) if m != 0 else None)
            if m == 0:
                I(lambda: te.matmul(pG[0][:, 0:128], sidn[:], sdi,
                                    start=False, stop=True),
                    "pe", ("tprime", 0))
        if probe:
            return

        for c in range(NCHUNK):
            if c == 0:
                W(te, "TE", "dve", ("tsb_done",))
            else:
                W(te, "TE", "dve", ("var2", c - 1))
            for t in range(DT):
                W(te, "TE", "dve", ("xb1g", c, t // 4))
                I(lambda c=c, t=t: te.matmul(
                    pS0[:], onesb[:], R2[:, R * t:R * (t + 1)],
                    start=(t == 0), stop=(t == DT - 1)),
                    "pe", ("sxmm1", c) if t == DT - 1 else None)
                W(te, "TE", "act", ("sq1", c, t))
                I(lambda c=c, t=t: te.matmul(
                    pS1[:], onesb[:], sqs[:, R * (t % 2):R * (t % 2) + R],
                    start=(t == 0), stop=(t == DT - 1)),
                    "pe", ("sqmm1", c, t))
            for i in range(DT):
                if i >= 2:
                    W(te, "TE", "dve", ("cevac", c, i - 2))
                elif c >= 1:
                    W(te, "TE", "dve", ("f2evac", c - 1, 30 + i))
                for j in range(i + 1):
                    W(te, "TE", "dve", ("xn1", c, j))
                    I(lambda c=c, i=i, j=j: te.matmul(
                        pD[i % 2][:],
                        tsb[:, 128 * (i - j):128 * (i - j) + 128],
                        R2[:, R * j:R * (j + 1)],
                        start=(j == 0), stop=(j == i)),
                        "pe", ("conv", c, i) if j == i else None)
            W(te, "TE", "dve", ("var1", c))
            for t in range(DT):
                W(te, "TE", "dve", ("xb2g", c, t // 4))
                I(lambda c=c, t=t: te.matmul(
                    pS0[:], onesb[:], R2[:, R * t:R * (t + 1)],
                    start=(t == 0), stop=(t == DT - 1)),
                    "pe", ("sxmm2", c) if t == DT - 1 else None)
                W(te, "TE", "act", ("sq2", c, t))
                I(lambda c=c, t=t: te.matmul(
                    pS1[:], onesb[:], sqs[:, R * (t % 2):R * (t % 2) + R],
                    start=(t == 0), stop=(t == DT - 1)),
                    "pe", ("sqmm2", c, t))
            W(te, "TE", "gp", ("xn2m", c, DT - 1))
            for f in range(FT):
                jg = 2 * (FT * c + f)
                W(te, "TE", "w", ("wg", c, f, 0))
                if f >= 2:
                    W(te, "TE", "act", ("silu", c, f - 2))
                elif c >= 1:
                    W(te, "TE", "act", ("silu", c - 1, 84 + f))
                for t in range(DT):
                    I(lambda f=f, t=t: te.matmul(
                        pG[f % 2][:], wgu[:, 128 * t:128 * (t + 1)],
                        R2[:, R * t:R * (t + 1)],
                        start=(t == 0), stop=(t == DT - 1)),
                        "pe", ("gate", c, f) if t == DT - 1 else None)
                mark("pe", ("gu", jg))
                W(te, "TE", "w", ("wg", c, f, 1))
                if f >= 2:
                    W(te, "TE", "dve", ("h", c, f - 2))
                elif c >= 1:
                    W(te, "TE", "dve", ("h", c - 1, 84 + f))
                for t in range(DT):
                    I(lambda f=f, t=t: te.matmul(
                        pU[f % 2][:], wgu[:, 4096 + 128 * t:4096 + 128 * (t + 1)],
                        R2[:, R * t:R * (t + 1)],
                        start=(t == 0), stop=(t == DT - 1)),
                        "pe", ("up", c, f) if t == DT - 1 else None)
                mark("pe", ("gu", jg + 1))
            W(te, "TE", "dve", ("h", c, FT - 1))
            for dt in range(DT):
                if dt >= 2:
                    W(te, "TE", "dve", ("f2evac", c, dt - 2))
                for hh in range(2):
                    kk = 2 * (DT * c + dt) + hh
                    W(te, "TE", "w", ("wd", c, dt, hh))
                    for l in range(FH):
                        fi = 43 * hh + l
                        I(lambda dt=dt, hh=hh, l=l, fi=fi: te.matmul(
                            pD[dt % 2][:],
                            wdn[:, 5504 * hh + 128 * l:5504 * hh + 128 * (l + 1)],
                            h_tile(fi),
                            start=(fi == 0), stop=(fi == FT - 1)),
                            "pe", ("down", c, dt) if fi == FT - 1 else None)
                    mark("pe", ("dn", kk))

    def body_v(v):
        def C(fn, sem="dve", label=None):
            I(fn, sem, label)
            if ctx["emit"]:
                v.wait_ge(sch.sems["dve"], sch.cnt["dve"])

        def ts(out, in0, s1, s2, o1, o2=None):
            if o2 is None:
                return lambda: v.tensor_scalar(out, in0, s1, s2, o1)
            return lambda: v.tensor_scalar(out, in0, s1, s2, o1, o2)

        W(v, "V", "ldc", ("all",))
        C(lambda: v.memset(epsc[:], EPS))
        W(v, "V", "pe", ("bre",))
        C(lambda: v.tensor_copy(prm[:, 10:11], pS0[0:64, 0:1]))
        W(v, "V", "pe", ("bim",))
        C(lambda: v.tensor_copy(prm[:, 11:12], pS0[0:64, 1:2]))
        W(v, "V", "pe", ("s2b",))
        C(ts(prm[:, 22:23], pS0[0:64, 2:3], 2.0 * SC32, None, ALU.mult, None))
        C(ts(prm[:, 23:24], prm[:, 0:1], SC32, None, ALU.mult, None))
        C(ts(prm[:, 24:25], prm[:, 1:2], SC32, None, ALU.mult, None))
        W(v, "V", "pe", ("db",))
        C(lambda: v.tensor_copy(dcol[:], pS0[0:128, 3:4]), "dve", ("dcol",))

        def cmul(ore, oim, a_re, a_im, b_re, b_im):
            C(lambda: v.tensor_mul(prm[:, 20:21], prm[:, a_re:a_re + 1],
                                   prm[:, b_re:b_re + 1]))
            C(lambda: v.tensor_mul(prm[:, 21:22], prm[:, a_im:a_im + 1],
                                   prm[:, b_im:b_im + 1]))
            C(lambda: v.tensor_add(prm[:, ore:ore + 1], prm[:, 20:21],
                                   prm[:, 21:22]))
            C(lambda: v.tensor_mul(prm[:, 20:21], prm[:, a_re:a_re + 1],
                                   prm[:, b_im:b_im + 1]))
            C(lambda: v.tensor_mul(prm[:, 21:22], prm[:, a_im:a_im + 1],
                                   prm[:, b_re:b_re + 1]))
            C(lambda: v.tensor_sub(prm[:, oim:oim + 1], prm[:, 20:21],
                                   prm[:, 21:22]))
        cmul(12, 13, 2, 3, 10, 11)
        cmul(14, 15, 2, 3, 4, 5)
        cmul(16, 17, 6, 7, 10, 11)
        cmul(18, 19, 6, 7, 4, 5)
        for i in range(4):
            re = 12 + 2 * i
            im = 13 + 2 * i
            C(ts(sV[0:64, 2 * i:2 * i + 1], prm[:, re:re + 1], SC32, None,
                 ALU.mult, None))
            C(ts(sV[64:128, 2 * i:2 * i + 1], prm[:, im:im + 1], SC32, None,
                 ALU.mult, None))
            C(ts(sV[0:64, 2 * i + 1:2 * i + 2], prm[:, im:im + 1], SC32, None,
                 ALU.mult, None))
            C(ts(sV[64:128, 2 * i + 1:2 * i + 2], prm[:, re:re + 1], -SC32,
                 None, ALU.mult, None))
        mark("dve", ("v_done",))

        g2re = S0(4096, 8192)[0:64, :]
        g2im = S0(8192, 12288)[0:64, :]
        s3 = S0(12288, 16384)[0:64, :]
        i0 = S0(0, 4096)[0:64, :]
        i1 = S0(0, 4096)[64:128, :]
        C(ts(g2re, g2re, prm[:, 22:23], prm[:, 23:24], ALU.mult, ALU.subtract))
        C(ts(g2im, g2im, prm[:, 22:23], prm[:, 24:25], ALU.mult, ALU.subtract))
        C(lambda: v.tensor_mul(s3, g2re, g2re))
        C(lambda: v.tensor_mul(i0, g2im, g2im))
        C(lambda: v.tensor_add(i0, i0, s3))
        C(lambda: v.reciprocal(s3, i0))
        C(lambda: v.tensor_mul(i0, g2re, s3))
        C(lambda: v.tensor_mul(i1, g2im, s3),
          "dve", ("inv",))
        for k in range(8):
            W(v, "V", "pe", ("cauchy", k))
            C(lambda k=k: v.tensor_copy(
                S0(4096 + 512 * k, 4096 + 512 * (k + 1))[0:8, :], pS1[0:8, :]),
                "dve", ("kv", k))

        W(v, "V", "kc", ("all",))
        C(ts(A_(2), KC(6), 1.0, None, ALU.add, None))
        C(lambda: v.tensor_mul(A_(3), A_(2), A_(2)))
        C(lambda: v.tensor_mul(A_(4), KC(7), KC(7)))
        C(lambda: v.tensor_add(A_(3), A_(3), A_(4)))
        C(lambda: v.reciprocal(A_(3), A_(3)))
        C(lambda: v.tensor_mul(A_(4), KC(2), KC(4)))
        C(lambda: v.tensor_mul(A_(5), KC(3), KC(5)))
        C(lambda: v.tensor_sub(A_(4), A_(4), A_(5)))
        C(lambda: v.tensor_mul(A_(5), KC(2), KC(5)))
        C(lambda: v.tensor_mul(A_(6), KC(3), KC(4)))
        C(lambda: v.tensor_add(A_(5), A_(5), A_(6)))
        C(lambda: v.tensor_mul(A_(6), A_(4), A_(2)))
        C(lambda: v.tensor_mul(A_(7), A_(5), KC(7)))
        C(lambda: v.tensor_add(A_(6), A_(6), A_(7)))
        C(lambda: v.tensor_mul(A_(6), A_(6), A_(3)))
        C(lambda: v.tensor_mul(A_(7), A_(5), A_(2)))
        C(lambda: v.tensor_mul(A_(8), A_(4), KC(7)))
        C(lambda: v.tensor_sub(A_(7), A_(7), A_(8)))
        C(lambda: v.tensor_mul(A_(7), A_(7), A_(3)))
        C(lambda: v.tensor_sub(A_(2), KC(0), A_(6)))
        C(lambda: v.tensor_sub(A_(3), KC(1), A_(7)))
        C(lambda: v.tensor_mul(A_(4), c64[:, 0:64], A_(2)))
        C(lambda: v.tensor_mul(A_(5), c64[:, 64:128], A_(3)))
        C(lambda: v.tensor_sub(A_(0), A_(4), A_(5)))
        C(lambda: v.tensor_mul(A_(4), c64[:, 0:64], A_(3)))
        C(lambda: v.tensor_mul(A_(5), c64[:, 64:128], A_(2)))
        C(lambda: v.tensor_add(A_(1), A_(4), A_(5)),
          "dve", ("ar_done",))
        C(lambda: v.memset(S0(0, 4224)[0:1, :], 0.0), "dve", ("zt",))
        W(v, "V", "pe", ("s1",))
        C(lambda: v.tensor_copy(A_(4), pS1[0:64, 0:64]))
        C(lambda: v.tensor_copy(A_(5), pS1[0:64, 64:128]))
        C(lambda: v.tensor_mul(A_(6), A_(4), c64[:, 320:384]))
        C(lambda: v.tensor_mul(A_(7), A_(5), c64[:, 384:448]))
        C(lambda: v.tensor_sub(A_(6), A_(6), A_(7)))
        C(lambda: v.tensor_mul(A_(7), A_(4), c64[:, 384:448]))
        C(lambda: v.tensor_mul(A_(8), A_(5), c64[:, 320:384]))
        C(lambda: v.tensor_add(A_(7), A_(7), A_(8)),
          "dve", ("tw_done",))
        W(v, "V", "pe", ("kmat",))
        C(ts(A_(9), pS1[0:64, 128:192], 1.0 / L, None, ALU.mult, None),
          "dve", ("ksb",))
        for m in range(DT):
            W(v, "V", "pe", ("tprime", m))
            C(lambda m=m: v.tensor_copy(
                tsb[:, 128 * m:128 * (m + 1)],
                pG[m % 2][:, 0:128]), "dve", ("tsb", m))
        mark("dve", ("tsb_done",))
        if probe:
            return

        def CI(fn, sem="dve", label=None):
            I(fn, sem, label)

        for c in range(NCHUNK):
            def stats(which):
                W(v, "V", "pe", ("sxmm%d" % which, c))
                W(v, "V", "pe", ("sqmm%d" % which, c, DT - 1))
                C(ts(m_rep[:], pS0[:], INV_D, None, ALU.mult, None))
                CI(ts(istd_rep[:], pS1[:], INV_D, None, ALU.mult, None))
                C(lambda: v.tensor_mul(tnorm[:, 2 * R:3 * R],
                                       m_rep[:], m_rep[:]))
                C(lambda: v.tensor_sub(istd_rep[:], istd_rep[:],
                                       tnorm[:, 2 * R:3 * R]),
                  "dve", ("var%d" % which, c))

            for g in range(8):
                if g == 0 and c >= 1:
                    W(v, "V", "pe", ("up", c - 1, FT - 1))
                W(v, "V", "x", ("x", c, g // 2))
                CI(lambda g=g: v.tensor_copy(
                    R2[:, 2048 * g:2048 * (g + 1)],
                    R1[:, 2048 * g:2048 * (g + 1)]),
                    "dve", ("xb1g", c, g))
            stats(1)
            for t in range(DT):
                CI(lambda t=t: v.tensor_sub(
                    R2[:, R * t:R * (t + 1)], R2[:, R * t:R * (t + 1)],
                    m_rep[:]), "dve", ("xn1", c, t))
            for i in range(DT):
                W(v, "V", "pe", ("conv", c, i))
                if i == 0:
                    W(v, "V", "act", ("istd1", c))
                C(lambda i=i: v.tensor_mul(
                    sc2[:, R * (i % 2):R * (i % 2) + R], pD[i % 2][:],
                    istd_rep[:]))
                CI(lambda i=i: v.tensor_add(
                    R1[:, R * i:R * (i + 1)],
                    sc2[:, R * (i % 2):R * (i % 2) + R],
                    R1[:, R * i:R * (i + 1)]), "dve", ("cevac", c, i))
            if True:
                if ctx["emit"]:
                    v.wait_ge(sch.sems["dve"], sch.cnt["dve"])
            for g in range(8):
                if g == 0:
                    W(v, "V", "pe", ("conv", c, DT - 1))
                CI(lambda g=g: v.tensor_copy(
                    R2[:, 2048 * g:2048 * (g + 1)],
                    R1[:, 2048 * g:2048 * (g + 1)]),
                    "dve", ("xb2g", c, g))
            stats(2)
            for t in range(DT):
                if t >= 2:
                    W(v, "V", "gp", ("xn2m", c, t - 2))
                CI(lambda t=t: v.tensor_sub(
                    tnorm[:, R * (t % 2):R * (t % 2) + R],
                    R2[:, R * t:R * (t + 1)], m_rep[:]),
                    "dve", ("xn2s", c, t))
            for f in range(FT):
                W(v, "V", "pe", ("up", c, f))
                W(v, "V", "act", ("silu", c, f))
                if f == 0:
                    W(v, "V", "xm", ("xmo", c))
                CI(lambda f=f: v.scalar_tensor_tensor(
                    h_tile(f), pU[f % 2][:], sus[:, f:f + 1],
                    silu_sb[:, R * (f % 2):R * (f % 2) + R],
                    ALU.mult, ALU.mult), "dve", ("h", c, f))
            for dt in range(DT):
                W(v, "V", "pe", ("down", c, dt))
                W(v, "V", "xm", ("xmi", c, dt))
                if dt >= 2:
                    W(v, "V", "out", ("out", c, dt - 2))
                CI(lambda dt=dt: v.scalar_tensor_tensor(
                    outs[:, R * (dt % 2):R * (dt % 2) + R], pD[dt % 2][:],
                    sds[:, dt:dt + 1], xms[:, R * (dt % 2):R * (dt % 2) + R],
                    ALU.mult, ALU.add), "dve", ("f2evac", c, dt))

    def body_a(a):
        def C(fn, sem="act", label=None):
            I(fn, sem, label)
            if ctx["emit"]:
                a.wait_ge(sch.sems["act"], sch.cnt["act"])

        W(a, "A", "ldc", ("all",))
        C(lambda: a.activation(st1[:, 2:3], st1[:, 1:2], AF.Exp, scale=-1.0),
          "act", ("exp",))
        W(a, "A", "dve", ("dcol",))
        C(lambda: a.activation(sdi, sidn[:], AF.Copy, scale=dcol[:]),
          "act", ("sdi",))
        if probe:
            return
        for c in range(NCHUNK):
            for t in range(DT):
                W(a, "A", "dve", ("xb1g", c, t // 4))
                if t >= 2:
                    W(a, "A", "pe", ("sqmm1", c, t - 2))
                C(lambda t=t: a.activation(
                    sqs[:, R * (t % 2):R * (t % 2) + R],
                    R2[:, R * t:R * (t + 1)], AF.Square),
                    "act", ("sq1", c, t))
            W(a, "A", "dve", ("var1", c))
            C(lambda: a.activation(tnorm[:, 2 * R:3 * R], istd_rep[:],
                                   AF.Ln, bias=epsc[:]))
            C(lambda: a.activation(istd_rep[:], tnorm[:, 2 * R:3 * R],
                                   AF.Exp, scale=-0.5),
              "act", ("istd1", c))
            for t in range(DT):
                W(a, "A", "dve", ("xb2g", c, t // 4))
                if t >= 2:
                    W(a, "A", "pe", ("sqmm2", c, t - 2))
                C(lambda t=t: a.activation(
                    sqs[:, R * (t % 2):R * (t % 2) + R],
                    R2[:, R * t:R * (t + 1)], AF.Square),
                    "act", ("sq2", c, t))
            W(a, "A", "dve", ("var2", c))
            C(lambda: a.activation(tnorm[:, 2 * R:3 * R], istd_rep[:],
                                   AF.Ln, bias=epsc[:]))
            C(lambda: a.activation(istd_rep[:], tnorm[:, 2 * R:3 * R],
                                   AF.Exp, scale=-0.5),
              "act", ("istd2", c))
            for f in range(FT):
                W(a, "A", "pe", ("gate", c, f))
                if f >= 2:
                    W(a, "A", "dve", ("h", c, f - 2))
                elif c >= 1:
                    W(a, "A", "dve", ("h", c - 1, 84 + f))
                C(lambda f=f: a.activation(
                    silu_sb[:, R * (f % 2):R * (f % 2) + R], pG[f % 2][:],
                    AF.Silu, scale=sgs[:, f:f + 1]),
                    "act", ("silu", c, f))

    for mode in ("count", "emit"):
        sch.mode = mode
        sch.reset()
        ctx["emit"] = mode == "emit"
        if mode == "count":
            body_sync(None)
            body_gq(None)
            body_te(None)
            body_v(None)
            body_a(None)
        else:
            with nc.Block() as block:
                block.sync(body_sync)
                block.gpsimd(body_gq)
                block.tensor(body_te)
                block.vector(body_v)
                block.scalar(body_a)
    es.close()
    return nc


_NC_CACHE = {}


def stage_inputs(L0_attn_u, L0_attn_Lambda_ri, L0_attn_p_ri, L0_attn_q_ri,
                 L0_attn_Vc_ri, L0_attn_Ct_ri, L0_attn_Bp, L0_attn_D,
                 L0_attn_log_step, L0_ffn_gate_w, L0_ffn_gate_s, L0_ffn_up_w,
                 L0_ffn_up_s, L0_ffn_down_w, L0_ffn_down_s):
    hc = host_constants()
    hw = host_weights(L0_ffn_gate_w, L0_ffn_up_w, L0_ffn_down_w,
                      L0_ffn_gate_s, L0_ffn_up_s, L0_ffn_down_s)
    lam = np.asarray(L0_attn_Lambda_ri, np.float32)
    p_ri = np.asarray(L0_attn_p_ri, np.float32)
    q_ri = np.asarray(L0_attn_q_ri, np.float32)
    vc = np.asarray(L0_attn_Vc_ri, np.float32)
    ct = np.asarray(L0_attn_Ct_ri, np.float32).reshape(64, 2)
    common = dict(
        guw=hw["guw"], dwt=hw["dwt"], gs=hw["gs"], us=hw["us"], ds=hw["ds"],
        lamre=np.ascontiguousarray(lam[:, 0:1]),
        lamim=np.ascontiguousarray(lam[:, 1:2]),
        ctre=np.ascontiguousarray(ct[:, 0:1]),
        ctim=np.ascontiguousarray(ct[:, 1:2]),
        pre_=np.ascontiguousarray(p_ri[:, 0:1]),
        pim=np.ascontiguousarray(p_ri[:, 1:2]),
        qre=np.ascontiguousarray(q_ri[:, 0:1]),
        qim=np.ascontiguousarray(q_ri[:, 1:2]),
        vctre=np.ascontiguousarray(vc[:, :, 0].T),
        vctim=np.ascontiguousarray(vc[:, :, 1].T),
        bp=np.asarray(L0_attn_Bp, np.float32).reshape(64, 1),
        dd=np.asarray(L0_attn_D, np.float32).reshape(1, 1),
        ls=np.asarray(L0_attn_log_step, np.float32).reshape(1, 1),
        g2re=hc["g2re"], g2im=hc["g2im"], cre=hc["cre"], cim=hc["cim"],
        w64re=hc["w64re"], w64im=hc["w64im"], w64imn=hc["w64imn"],
        twre=hc["twre"], twim=hc["twim"], onesf=hc["onesf"],
        onesb=hc["onesb"],
        idn=hc["idn"], jfl=hc["jfl"],
    )
    xt = np.ascontiguousarray(
        np.asarray(L0_attn_u, np.float32).reshape(-1, D).T)
    return common, xt


def kernel(**inputs):
    common, xt = stage_inputs(**inputs)
    in_maps = []
    for c in range(NCORES):
        m = dict(common)
        m["x"] = np.ascontiguousarray(xt[:, RPC * c:RPC * (c + 1)])
        in_maps.append(m)
    if "main" not in _NC_CACHE:
        _NC_CACHE["main"] = build_nc(probe=False)
    nc = _NC_CACHE["main"]
    res = run_bass_kernel_spmd(nc, in_maps, core_ids=list(range(NCORES)))
    full = np.concatenate([r["out"] for r in res.results], axis=1)
    return np.ascontiguousarray(full.T).reshape(
        np.asarray(inputs["L0_attn_u"]).shape).astype(np.float32)


if __name__ == "__main__":
    nc = build_nc()
    print("instructions:", sum(1 for _ in nc.all_instructions()))
